# version 07
# baseline (speedup 1.0000x reference)
import numpy as np
import concourse.bass as bass
import concourse.mybir as mybir
from concourse.bass_utils import run_bass_kernel_spmd
from concourse.ap import AP

F32 = mybir.dt.float32
BF16 = mybir.dt.bfloat16
U32 = mybir.dt.uint32
AF = mybir.ActivationFunctionType
ALU = mybir.AluOpType
AX = mybir.AxisListType

T = 2048
TH = 1024
TQ = 512
D = 1024
NK = 8
NT = 16
NE = 8
EPS_LN = 1e-5
EPS_RMS = 1e-6
NEG = -1e30


class Sem:
    def __init__(self, nc, name):
        self.h = nc.alloc_semaphore(name)
        self.v = 0


class Builder:
    def __init__(self, debug=False, stage=99, sub=99):
        self.stage = stage
        self.sub = sub
        self.debug = debug
        nc = self.nc = bass.Bass("TRN2", target_bir_lowering=False)
        self.ops = {e: [] for e in ("tensor", "vector", "scalar", "gpsimd", "sync")}
        self.declare_io()
        self.declare_sbuf()
        self.sems()
        self.walk()

    def on(self, eng, fn):
        self.ops[eng].append(fn)

    def t(self, fn, inc=0):
        s = self.s_pe
        if inc:
            self.on("tensor", lambda e, fn=fn: (fn(e).then_inc(s.h, inc)))
            s.v += inc
        else:
            self.on("tensor", fn)

    def v(self, fn, inc=0):
        s = self.s_dv
        if inc:
            self.on("vector", lambda e, fn=fn: (fn(e).then_inc(s.h, inc)))
            s.v += inc
        else:
            self.on("vector", fn)

    def sc(self, fn, inc=0):
        s = self.s_sc
        if inc:
            self.on("scalar", lambda e, fn=fn: (fn(e).then_inc(s.h, inc)))
            s.v += inc
        else:
            self.on("scalar", fn)

    def gp(self, fn, inc=0):
        s = self.s_gp
        if inc:
            self.on("gpsimd", lambda e, fn=fn: (fn(e).then_inc(s.h, inc)))
            s.v += inc
        else:
            self.on("gpsimd", fn)

    def wdma(self, out, in_, guard=None):
        if guard is not None:
            self.wait("gpsimd", self.s_pe, self.bfree.get(guard, 0))
        s = self.s_w
        self.on("gpsimd", lambda e, o=out, i=in_: e.dma_start(out=o, in_=i).then_inc(s.h, 16))
        s.v += 16
        return s.v

    def free_buf(self, name):
        self.bfree[name] = self.s_pe.v

    def xdma(self, out, in_):
        s = self.s_x
        self.on("sync", lambda e, o=out, i=in_: e.dma_start(out=o, in_=i).then_inc(s.h, 16))
        s.v += 16
        return s.v

    def odma(self, out, in_):
        s = self.s_out
        self.on("sync", lambda e, o=out, i=in_: e.dma_start(out=o, in_=i).then_inc(s.h, 16))
        s.v += 16
        return s.v


    def _retro(self, eng, sem, inc):
        ops = self.ops[eng]
        prev = ops[-1]

        def wrapped(e, prev=prev, sem=sem, inc=inc):
            inst = prev(e)
            inst.then_inc(sem.h, inc)
            return inst
        ops[-1] = wrapped
        sem.v += inc

    def vS(self, fn):
        self.v(fn, inc=1)
        self.wait("vector", self.s_dv)

    def wait(self, eng, sem, val=None):
        vv = sem.v if val is None else val
        if vv > 0:
            self.on(eng, lambda e, s=sem.h, vv=vv: e.wait_ge(s, vv))

    def declare_io(self):
        nc = self.nc
        dt_ = nc.dram_tensor
        self.x_c = dt_("x_c", [T, D], F32, kind="ExternalInput").ap()
        self.halo_x = dt_("halo_x", [128, D], F32, kind="ExternalInput").ap()
        self.w_in_gate = dt_("w_in_gate", [16, 128, NK, 128], F32, kind="ExternalInput").ap()
        self.w_in_ssm = dt_("w_in_ssm", [16, 128, NK, 128], F32, kind="ExternalInput").ap()
        self.w_in_bc = dt_("w_in_bc", [128, NK, 128], F32, kind="ExternalInput").ap()
        self.w_in_dt = dt_("w_in_dt", [128, NK, 32], F32, kind="ExternalInput").ap()
        self.w_out_t = dt_("w_out_t", [8, 128, 16, 128], F32, kind="ExternalInput").ap()
        self.W1_t = dt_("W1_t", [NE, 128, 4, NK, 128], F32, kind="ExternalInput").ap()
        self.W2_t = dt_("W2_t", [8, 128, NE, 4, 128], F32, kind="ExternalInput").ap()
        self.Ws1_t = dt_("Ws1_t", [4, 128, NK, 128], F32, kind="ExternalInput").ap()
        self.Ws2_t = dt_("Ws2_t", [8, 128, 4, 128], F32, kind="ExternalInput").ap()
        self.gateW_t = dt_("gateW_t", [128, NK, 8], F32, kind="ExternalInput").ap()
        self.b2_in = dt_("b2_in", [NE, D], F32, kind="ExternalInput").ap()
        self.b1T_in = dt_("b1T_in", [128, 32], F32, kind="ExternalInput").ap()
        self.bs1T_in = dt_("bs1T_in", [128, 4], F32, kind="ExternalInput").ap()
        self.bs2T_in = dt_("bs2T_in", [128, 8], F32, kind="ExternalInput").ap()
        self.convT_in = dt_("convT_in", [128, 64], F32, kind="ExternalInput").ap()
        self.cbT_in = dt_("cbT_in", [128, 16], F32, kind="ExternalInput").ap()
        self.nsT_in = dt_("nsT_in", [128, 8], F32, kind="ExternalInput").ap()
        self.dtb_in = dt_("dtb_in", [32, 1], F32, kind="ExternalInput").ap()
        self.alog_in = dt_("alog_in", [32, 1], F32, kind="ExternalInput").ap()
        self.rs_in = dt_("rs_in", [1, 1], F32, kind="ExternalInput").ap()
        self.out_c = dt_("out_c", [T, D], F32, kind="ExternalOutput").ap()

        import ml_dtypes
        lt_np = np.triu(np.ones((128, 128), np.float32))
        self.eye_f32_d = nc.inline_tensor(np.eye(128, dtype=np.float32), "eye_f32_d").ap()
        self.eye_bf_d = nc.inline_tensor(np.eye(128, dtype=ml_dtypes.bfloat16), "eye_bf_d").ap()
        self.lt_d = nc.inline_tensor(lt_np, "lt_d").ap()
        self.ones_d = nc.inline_tensor(np.ones((1, 128), np.float32), "ones_d").ap()
        fc = np.zeros((128, 8), np.float32)
        fc[:, 0] = EPS_LN; fc[:, 1] = EPS_RMS; fc[:, 2] = 1.0; fc[:, 3] = -1.0
        fc[:, 4] = NEG; fc[:, 5] = 0.5
        self.fconst_d = nc.inline_tensor(fc, "fconst_d").ap()

        if self.debug:
            self.dbg = {}
            for name, shape, dty in [
                ("d_stats", [128, NT * 4], F32),
                ("d_xm", [128, 8 * NK * 128], BF16),
                ("d_bc", [128, T], F32),
                ("d_dt", [32, T], F32),
                ("d_csa", [128, 16 * 32], F32),
                ("d_h", [128, TH], F32),
                ("d_gate", [128, TH], F32),
                ("d_ybf", [128, 16 * TH], BF16),
                ("d_yout", [128, 8 * T], BF16),
                ("d_scores", [128, TQ], F32),
                ("d_gq", [128, 8 * TQ], F32),
                ("d_w0", [128, 8 * T], BF16),
                ("d_xn", [128, 4096], BF16),
                ("d_top", [128, 1024], F32),
            ]:
                self.dbg[name] = nc.dram_tensor(name, shape, dty, kind="ExternalOutput").ap()

    def declare_sbuf(self):
        nc = self.nc
        al = nc.alloc_sbuf_tensor
        self.eye_f = al("eye_f", [128, 128], F32)
        self.eye_b = al("eye_b", [128, 128], BF16)
        self.lt_sb = al("lt_sb", [128, 128], F32)
        self.ones_sb = al("ones_sb", [1, 128], F32)
        self.fconst = al("fconst", [128, 8], F32)
        self.b1T = al("b1T", [128, 32], F32)
        self.bs1T = al("bs1T", [128, 4], F32)
        self.bs2T = al("bs2T", [128, 16], F32)
        self.convT = al("convT", [128, 64], F32)
        self.cbT = al("cbT", [128, 16], F32)
        self.nsT = al("nsT", [128, 8], F32)
        self.dtb = al("dtb", [32, 1], F32)
        self.alog = al("alog", [32, 1], F32)
        self.acol = al("acol", [32, 4], F32)
        self.rs_sb = al("rs_sb", [8, 2], F32)
        self.rs128 = al("rs128", [128, 1], F32)
        self.b2_sb = al("b2_sb", [8, D], F32)
        self.gateW_sb = al("gateW_sb", [128, NK * 8], BF16)
        self.wdt_sb = al("wdt_sb", [128, NK * 32], BF16)
        self.stats = al("stats", [128, NT * 4], F32)
        self.xm_fm = al("xm_fm", [128, 8 * NK * 128], BF16)
        self.xm_halo = al("xm_halo", [128, NK * 128], BF16)
        self.y_out = al("y_out", [128, 8 * T], BF16)
        self.bigbuf = al("bigbuf", [128, 16384], BF16)
        self.g_q = al("g_q", [128, 8 * TQ], F32)
        self.BC_fm = al("BC_fm", [128, TH], F32)
        self.B_tm = al("B_tm", [128, 16 * 64], BF16)
        self.C_tm = al("C_tm", [128, 16 * 64], BF16)
        self.csA_all = al("csA_all", [128, 16 * 32], F32)
        self.dt_tm = al("dt_tm", [128, 16 * 32], BF16)
        self.eneg = al("eneg", [128, 16 * 32], BF16)
        self.epos = al("epos", [128, 16 * 32], BF16)
        self.r32 = al("r32", [32, 3072], F32)
        self.small32 = al("small32", [128, 64], F32)
        self.halo_ssm = al("halo_ssm", [128, 16 * 4], F32)
        self.tail3 = al("tail3", [128, 16 * 4], F32)
        self.states_all = al("states_all", [64, 16 * 128], F32)
        self.states_bf = al("states_bf", [64, 128], BF16)
        self.st_scr = al("st_scr", [64, 128], F32)
        self.dpcol = al("dpcol", [64, 32], F32)
        self.scr_in = al("scr_in", [128, TH + 3], F32)
        self.conv_acc = al("conv_acc", [128, TH], F32)
        self.gate_sb = al("gate_sb", [128, TH], F32)
        self.xdtT = al("xdtT", [128, 8 * 128], BF16)
        self.ben_tm = al("ben_tm", [128, 8 * 128], BF16)
        self.cse_tm = al("cse_tm", [128, 8 * 128], BF16)
        self.ben_fm = al("ben_fm", [64, 16 * 128], BF16)
        self.cse_fm = al("cse_fm", [64, 16 * 128], BF16)
        self.w_ssd = al("w_ssd", [128, 8 * 256], BF16)
        self.h_bf = None
        self.xtile = [al("xtile0", [128, D], F32), al("xtile1", [128, D], F32)]
        self.xn_t = al("xn_t", [128, D], F32)
        self.col = al("colscr", [128, 16], F32)
        self.tk = al("tk", [128, 32], F32)
        self.h_sb = None
        self.wbuf = [al("wbuf0", [128, 1024], BF16), al("wbuf1", [128, 1024], BF16),
                     al("wbuf2", [128, 1024], BF16), al("wbuf3", [128, 1024], BF16)]
        self.w2buf = [al("w2buf0", [128, 4096], BF16), al("w2buf1", [128, 4096], BF16)]
        self.wsbuf = [al("wsbuf0", [128, 512], BF16), None]
        self.P = [nc.alloc_psum_tensor(f"pb{i}", [128, 512], F32) for i in range(8)]

    def pb(self, i):
        return self.P[i].ap()

    def pbf(self, i):
        return self.P[i].ap().bitcast(BF16)

    def sems(self):
        self.bfree = {}
        nc = self.nc
        self.s_x = Sem(nc, "s_x")
        self.s_w = Sem(nc, "s_w")
        self.s_gp = Sem(nc, "s_gp")
        self.s_pe = Sem(nc, "s_pe")
        self.s_dv = Sem(nc, "s_dv")
        self.s_sc = Sem(nc, "s_sc")
        self.s_out = Sem(nc, "s_out")

    def walk(self):
        st = self.stage
        self.setup()
        if st >= 1:
            for half in range(2):
                if st < 2 and half > 0:
                    break
                self.xm_rebuild(half)
                if st >= 2:
                    self.b0_half(half)
                if st >= 3:
                    for m in range(16 if st >= 4 else 1):
                        self.mamba_m(half, m)
                if st >= 5:
                    self.w_out_half(half)
                if st < 6 and half == 0:
                    break
        if st >= 6:
            for q in range(4 if st >= 7 else 1):
                self.ffn_quarter(q)
        self.wait("sync", self.s_out)
        self.wait("gpsimd", self.s_w)
        self.wait("gpsimd", self.s_gp)
        self.emit()

    def emit(self):
        with self.nc.Block() as block:
            for name in ("tensor", "vector", "scalar", "gpsimd", "sync"):
                ops = self.ops[name]

                def body(eng, ops=ops):
                    for fn in ops:
                        fn(eng)
                getattr(block, name)(body)

    def setup(self):
        for dst, src in [(self.eye_f, self.eye_f32_d), (self.eye_b, self.eye_bf_d),
                         (self.lt_sb, self.lt_d), (self.ones_sb, self.ones_d),
                         (self.fconst, self.fconst_d),
                         (self.b1T, self.b1T_in), (self.bs1T, self.bs1T_in),
                         (self.bs2T[0:128, 0:8], self.bs2T_in), (self.convT, self.convT_in),
                         (self.cbT, self.cbT_in), (self.nsT, self.nsT_in),
                         (self.dtb, self.dtb_in), (self.alog, self.alog_in),
                         (self.b2_sb, self.b2_in)]:
            self.xdma(dst[:], src)
        self.xdma(self.rs_sb[0:1, 0:1], self.rs_in)
        self.wdma(self.gateW_sb[:], self.gateW_t.rearrange("p k n -> p (k n)"))
        self.wdma(self.wdt_sb[:], self.w_in_dt.rearrange("p k n -> p (k n)"))
        ld = self.s_x.v
        self.wait("scalar", self.s_x, ld)
        self.sc(lambda e: e.activation(out=self.acol[0:32, 0:1], in_=self.alog[:], func=AF.Exp), inc=1)
        self.wait("vector", self.s_sc)
        self.v(lambda e: e.tensor_scalar_mul(self.acol[0:32, 1:2], self.acol[0:32, 0:1], -1.0))
        self.v(lambda e: e.tensor_scalar_mul(self.acol[0:32, 2:3], self.dtb[:], -1.0))
        self.v(lambda e: e.tensor_scalar_mul(self.bs2T[:, 8:16], self.bs2T[:, 0:8], -1.0), inc=1)
        self.wait("tensor", self.s_x, ld)
        self.t(lambda e: e.matmul(self.pb(6)[:, 0:1], self.ones_sb[0:1, :],
                                  self.rs_sb[0:1, 0:1], skip_group_check=True, start=True, stop=True), inc=1)
        self.wait("vector", self.s_pe)
        self.v(lambda e: e.tensor_copy(self.rs128[:, 0:1], self.pb(6)[:, 0:1]), inc=1)
        self.wait("tensor", self.s_dv)

    def ln_tile(self, xbuf, store_stats_at=None):
        col = self.col
        self.wait("scalar", self.s_dv)
        self.sc(lambda e: e.activation(out=self.xn_t[:], in_=xbuf[:], func=AF.Square,
                                       accum_out=col[:, 0:1]))
        self.sc(lambda e: e.activation(out=self.xn_t[:], in_=xbuf[:], func=AF.Identity,
                                       accum_out=col[:, 1:2]), inc=1)
        self.wait("vector", self.s_sc)
        self.v(lambda e: e.tensor_scalar_mul(col[:, 2:3], col[:, 1:2], 1.0 / D))
        self.vS(lambda e: e.tensor_scalar_mul(col[:, 3:4], col[:, 0:1], 1.0 / D))
        self.vS(lambda e: e.tensor_mul(col[:, 4:5], col[:, 2:3], col[:, 2:3]))
        self.v(lambda e: e.tensor_sub(col[:, 5:6], col[:, 3:4], col[:, 4:5]), inc=1)
        self.wait("scalar", self.s_dv)
        self.sc(lambda e: e.activation(out=col[:, 6:7], in_=col[:, 5:6], func=AF.Sqrt,
                                       bias=self.fconst[:, 0:1]), inc=1)
        self.wait("vector", self.s_sc)
        self.vS(lambda e: e.tensor_scalar_add(col[:, 8:9], col[:, 5:6], EPS_LN))
        self.v(lambda e: e.reciprocal(col[:, 7:8], col[:, 6:7]))
        self.vS(lambda e: e.reciprocal(col[:, 9:10], col[:, 8:9]))
        self.v(lambda e: e.tensor_mul(col[:, 10:11], col[:, 5:6], col[:, 9:10]), inc=1)
        self.wait("scalar", self.s_dv)
        self.sc(lambda e: e.activation(out=col[:, 11:12], in_=col[:, 10:11], func=AF.Sqrt,
                                       bias=self.fconst[:, 1:2]), inc=1)
        self.wait("vector", self.s_sc)
        self.vS(lambda e: e.reciprocal(col[:, 12:13], col[:, 11:12]))
        self.vS(lambda e: e.tensor_mul(col[:, 13:14], col[:, 7:8], col[:, 12:13]))
        if store_stats_at is not None:
            st = store_stats_at
            self.v(lambda e: e.tensor_copy(self.stats[:, st:st + 1], col[:, 2:3]))
            self.v(lambda e: e.tensor_copy(self.stats[:, st + 1:st + 2], col[:, 7:8]))
            self.v(lambda e: e.tensor_copy(self.stats[:, st + 2:st + 3], col[:, 13:14]))
            if self.debug:
                self.v(lambda e: e.tensor_copy(self.stats[:, st + 3:st + 4], col[:, 0:1]))

    def xm_rebuild(self, half):
        tiles = list(range(8 * half, 8 * half + 8))
        if half == 0:
            tiles.append(-1)
        for i, tt in enumerate(tiles):
            xbuf = self.xtile[i % 2]
            src = self.halo_x if tt < 0 else self.x_c[tt * 128:(tt + 1) * 128, :]
            if i >= 2:
                self.wait("sync", self.s_dv)
                self.wait("sync", self.s_sc)
            ld = self.xdma(xbuf[:], src)
            self.wait("scalar", self.s_x, ld)
            self.wait("vector", self.s_x, ld)
            self.ln_tile(xbuf, store_stats_at=None if tt < 0 else 4 * tt)
            if self.sub < 2:
                continue
            col = self.col
            self.v(lambda e, xb=xbuf: e.tensor_scalar(self.xn_t[:], xb[:], col[:, 2:3],
                                                      col[:, 7:8], ALU.subtract, ALU.mult))
            self.v(lambda e: e.tensor_scalar_mul(self.w_ssd[:, 1024:2048], self.xn_t[:], col[:, 12:13]), inc=1)
            dvv = self.s_dv.v
            if self.sub < 3:
                continue
            self.wait("tensor", self.s_dv, dvv)
            if i > 0:
                self.wait("tensor", self.s_dv)
            for k in range(NK):
                self.t(lambda e, k=k: e.transpose(self.pbf(0)[:, 128 * k:128 * (k + 1)],
                                                  self.w_ssd[:, 1024 + 128 * k:1024 + 128 * (k + 1)],
                                                  self.eye_b[:]))
            self._retro("tensor", self.s_pe, 1)
            self.wait("vector", self.s_pe)
            if self.sub < 4:
                continue
            self.v(lambda e: e.tensor_copy(AP(self.w_ssd.ap().tensor, 1024, [[2048, 128], [1, 1024]]).bitcast(U32),
                                           self.pbf(0).bitcast(U32)), inc=1)
            if self.sub < 5:
                continue
            for k in range(NK):
                if tt < 0:
                    dst = self.xm_halo[:, 128 * k:128 * (k + 1)]
                else:
                    tl = tt - 8 * half
                    dst = self.xm_fm[:, tl * 1024 + 128 * k: tl * 1024 + 128 * (k + 1)]
                self.v(lambda e, k=k, dst=dst: e.tensor_scalar_mul(
                    dst, self.w_ssd[:, 1024 + 128 * k:1024 + 128 * (k + 1)], self.nsT[:, k:k + 1]),
                    inc=(1 if k == NK - 1 else 0))
        if self.debug and half == 0:
            self.wait("sync", self.s_dv)
            self.odma(self.dbg["d_xm"], self.xm_fm[:])
            self.odma(self.dbg["d_stats"], self.stats[:])

    def rhs_xm(self, k, n4):
        return AP(self.xm_fm.ap().tensor, 4 * n4 * 1024 + 128 * k,
                  [[8192, 128], [1024, 4], [1, 128]])

    def b0_half(self, half):
        pb, pbf = self.pb, self.pbf
        wv = self.wdma(self.wbuf[0][:], self.w_in_bc.rearrange("p k n -> p (k n)"),
                       guard="wbuf0")
        self.wait("tensor", self.s_w, wv)
        self.wait("tensor", self.s_dv)
        self.wait("tensor", self.s_sc)
        for k in range(NK):
            for n2 in range(2):
                self.t(lambda e, k=k, n2=n2: e.matmul(
                    pb(n2)[:], self.wbuf[0][:, 128 * k:128 * (k + 1)],
                    self.rhs_xm(k, n2), skip_group_check=True, start=(k == 0), stop=(k == NK - 1)))
                self.t(lambda e, k=k, n2=n2: e.matmul(
                    pb(2 + n2)[0:32, :], self.wdt_sb[:, 32 * k:32 * (k + 1)],
                    self.rhs_xm(k, n2), skip_group_check=True, start=(k == 0), stop=(k == NK - 1)))
        self._retro("tensor", self.s_pe, 1)
        self.free_buf("wbuf0")
        self.wait("vector", self.s_pe)
        self.wait("scalar", self.s_pe)
        for n2 in range(2):
            self.v(lambda e, n2=n2: e.tensor_copy(self.BC_fm[:, 512 * n2:512 * (n2 + 1)], pb(n2)[:]))
            self.sc(lambda e, n2=n2: e.activation(out=self.r32[0:32, 512 * n2:512 * (n2 + 1)],
                                                  in_=pb(2 + n2)[0:32, :], func=AF.Identity,
                                                  bias=self.dtb[:]), inc=(n2 == 1))
        self._retro("vector", self.s_dv, 1)
        self.wait("tensor", self.s_dv)
        self.wait("tensor", self.s_sc)
        if self.debug and half == 0:
            self.wait("sync", self.s_dv)
            self.odma(self.dbg["d_bc"][:, 0:TH], self.BC_fm[:])

        hs = slice(1024 * half, 1024 * half + TH)
        X = self.r32[0:32, 0:1024]
        SCR = self.r32[0:32, 1024:2048]
        AD = self.r32[0:32, 2048:3072]
        if self.debug and half == 0:
            self.wait("sync", self.s_sc)
            self.odma(self.dbg["d_dt"][:, 1024:2048], X)
            self.wait("scalar", self.s_out)
            self.wait("vector", self.s_out)
        self.sc(lambda e, X=X, SCR=SCR: e.activation(out=SCR, in_=X, func=AF.Abs))
        self.sc(lambda e, SCR=SCR, AD=AD: e.activation(out=AD, in_=SCR, func=AF.Exp,
                                                       scale=self.fconst[0:32, 3:4]))
        self.sc(lambda e, SCR=SCR, AD=AD: e.activation(out=SCR, in_=AD, func=AF.Ln,
                                                       bias=self.fconst[0:32, 2:3]))
        self.sc(lambda e, X=X, AD=AD: e.activation(out=AD, in_=X, func=AF.Relu), inc=1)
        self.wait("vector", self.s_sc)
        self.v(lambda e, X=X, SCR=SCR, AD=AD: e.tensor_add(X, SCR, AD), inc=1)
        self.wait("scalar", self.s_dv)
        self.sc(lambda e, X=X, AD=AD: e.activation(out=AD, in_=X, func=AF.Identity,
                                                   scale=self.acol[0:32, 1:2]), inc=1)
        dts, adts = X, AD
        self.wait("vector", self.s_sc)
        self._retro("vector", self.s_dv, 1)
        dv_adt = self.s_dv.v
        if self.debug:
            self.wait("sync", self.s_dv, dv_adt)
            self.odma(self.dbg["d_dt"][:, hs], dts)

        self.wait("tensor", self.s_dv, dv_adt)
        for tl in range(8):
            self.t(lambda e, tl=tl: e.transpose(pb(0)[:, 32 * tl:32 * (tl + 1)],
                                                adts[:, 128 * tl:128 * (tl + 1)],
                                                self.eye_f[0:32, 0:32]))
            self.t(lambda e, tl=tl: e.transpose(pb(1)[:, 32 * tl:32 * (tl + 1)],
                                                dts[:, 128 * tl:128 * (tl + 1)],
                                                self.eye_f[0:32, 0:32]))
        self._retro("tensor", self.s_pe, 1)
        self.wait("vector", self.s_pe)
        self.v(lambda e: e.tensor_copy(self.conv_acc[:, 0:256], pb(0)[:, 0:256]))
        self.v(lambda e: e.tensor_copy(self.dt_tm[:, 32 * 8 * half:32 * 8 * (half + 1)],
                                       pb(1)[:, 0:256]), inc=1)
        self.wait("tensor", self.s_dv)
        for tl in range(8):
            self.t(lambda e, tl=tl: e.matmul(pb(2)[:, 32 * tl:32 * (tl + 1)], self.lt_sb[:],
                                             self.conv_acc[:, 32 * tl:32 * (tl + 1)],
                                             skip_group_check=True, start=True, stop=True))
        self._retro("tensor", self.s_pe, 1)
        self.wait("vector", self.s_pe)
        cs = slice(32 * 8 * half, 32 * 8 * (half + 1))
        self.v(lambda e: e.tensor_copy(self.csA_all[:, cs], pb(2)[:, 0:256]), inc=1)
        self.wait("tensor", self.s_dv)
        if self.debug:
            self.wait("sync", self.s_dv)
            self.odma(self.dbg["d_csa"][:, cs], self.csA_all[:, cs])
        for tl in range(8):
            self.t(lambda e, tl=tl: e.transpose(
                pb(3 + tl // 4)[0:32, 128 * (tl % 4):128 * (tl % 4 + 1)],
                self.csA_all[:, 32 * (8 * half + tl):32 * (8 * half + tl + 1)],
                self.eye_f[:]))
        self._retro("tensor", self.s_pe, 1)
        self.wait("vector", self.s_pe)
        self.v(lambda e: e.tensor_copy(self.r32[0:32, 2048:2560], pb(3)[0:32, :]))
        self.v(lambda e: e.tensor_copy(self.r32[0:32, 2560:3072], pb(4)[0:32, :]), inc=1)
        self.wait("tensor", self.s_dv)
        S32 = self.small32
        refc = S32[0:32, 0:16]
        Tc = S32[0:32, 16:32]
        hh = 8 * half
        self.v(lambda e: e.tensor_copy(
            AP(S32.ap().tensor, 0 + hh, [[64, 32], [1, 8]]),
            AP(self.r32.ap().tensor, 2048 + 63, [[3072, 32], [128, 8]])))
        self.v(lambda e: e.tensor_copy(
            AP(S32.ap().tensor, 16 + hh, [[64, 32], [1, 8]]),
            AP(self.r32.ap().tensor, 2048 + 127, [[3072, 32], [128, 8]])), inc=1)
        dpa = S32[0:32, 32:48]
        dpv = S32[0:32, 48:64]
        c0 = max(0, hh - 1)
        c1 = hh + 7
        n = c1 - c0
        self.vS(lambda e, c0=c0, n=n: e.tensor_copy(dpa[:, c0:c0 + n], refc[:, c0 + 1:c0 + 1 + n]))
        self.vS(lambda e, c0=c0, n=n: e.tensor_tensor(dpa[:, c0:c0 + n], dpa[:, c0:c0 + n],
                                                      refc[:, c0:c0 + n], ALU.subtract))
        self.v(lambda e, c0=c0, n=n: e.tensor_tensor(dpa[:, c0:c0 + n], dpa[:, c0:c0 + n],
                                                     Tc[:, c0:c0 + n], ALU.add), inc=1)
        self.wait("scalar", self.s_dv)
        self.sc(lambda e, c0=c0, n=n: e.activation(out=dpv[:, c0:c0 + n], in_=dpa[:, c0:c0 + n],
                                                   func=AF.Exp,
                                                   scale=self.fconst[0:32, 5:6]), inc=1)
        self.wait("gpsimd", self.s_sc)
        self.wait("gpsimd", self.s_dv)
        self.gp(lambda e: e.dma_start(
            out=AP(self.r32.ap().tensor, 0, [[3072, 1], [16, 32], [1, 16]]),
            in_=AP(self.small32.ap().tensor, 48, [[64, 32], [1, 16]])), inc=16)
        self.gp(lambda e: e.dma_start(
            out=AP(self.r32.ap().tensor, 512, [[3072, 1], [16, 32], [1, 16]]),
            in_=AP(self.small32.ap().tensor, 0, [[64, 32], [1, 16]])), inc=16)
        gp_flat = self.s_gp.v
        self.wait("tensor", self.s_gp, gp_flat)
        for tl in range(8):
            tc = 8 * half + tl
            rhs = AP(self.r32.ap().tensor, 512 + tc, [[3072, 1], [16, 32]])
            self.t(lambda e, tl=tl, rhs=rhs: e.matmul(
                pb(5)[:, 32 * tl:32 * (tl + 1)], self.ones_sb[0:1, :], rhs,
                skip_group_check=True, start=True, stop=True))
        self._retro("tensor", self.s_pe, 1)
        self.wait("vector", self.s_pe)
        self.v(lambda e: e.tensor_tensor(self.conv_acc[:, 0:256], pb(5)[:, 0:256], self.csA_all[:, cs],
                                         ALU.subtract), inc=1)
        self.wait("tensor", self.s_dv)
        self.wait("scalar", self.s_dv)
        self.sc(lambda e: e.activation(out=self.eneg[:, cs], in_=self.conv_acc[:, 0:256], func=AF.Exp))
        self.sc(lambda e: e.activation(out=self.epos[:, cs], in_=self.conv_acc[:, 0:256], func=AF.Exp,
                                       scale=self.fconst[:, 3:4]), inc=1)
        self.wait("gpsimd", self.s_dv)
        self.gp(lambda e: e.dma_start(out=self.gate_sb[0:64, :], in_=self.BC_fm[64:128, :]),
                inc=16)
        self.wait("tensor", self.s_gp)
        for tl in range(8):
            bk6 = 6 + tl // 4
            self.t(lambda e, tl=tl, bk6=bk6: e.transpose(
                pb(bk6)[:, 64 * (tl % 4):64 * (tl % 4 + 1)],
                self.BC_fm[0:64, 128 * tl:128 * (tl + 1)], self.eye_f[0:64, 0:64]))
        self._retro("tensor", self.s_pe, 1)
        self.wait("vector", self.s_pe)
        self.v(lambda e: e.tensor_copy(self.B_tm[:, 64 * 8 * half:64 * 8 * half + 256], pb(6)[:, 0:256]))
        self.v(lambda e: e.tensor_copy(self.B_tm[:, 64 * 8 * half + 256:64 * 8 * (half + 1)],
                                       pb(7)[:, 0:256]), inc=1)
        self.wait("tensor", self.s_dv)
        for tl in range(8):
            bk6 = 6 + tl // 4
            self.t(lambda e, tl=tl, bk6=bk6: e.transpose(
                pb(bk6)[:, 64 * (tl % 4):64 * (tl % 4 + 1)],
                self.gate_sb[0:64, 128 * tl:128 * (tl + 1)], self.eye_f[0:64, 0:64]))
        self._retro("tensor", self.s_pe, 1)
        self.wait("vector", self.s_pe)
        self.v(lambda e: e.tensor_copy(self.C_tm[:, 64 * 8 * half:64 * 8 * half + 256], pb(6)[:, 0:256]))
        self.v(lambda e: e.tensor_copy(self.C_tm[:, 64 * 8 * half + 256:64 * 8 * (half + 1)],
                                       pb(7)[:, 0:256]), inc=1)
        self.wait("tensor", self.s_dv)

    def mamba_m(self, half, m):
        pb, pbf = self.pb, self.pbf
        wi = self.wbuf[2 * (m % 2)]
        wg = self.wbuf[2 * (m % 2) + 1]
        win = f"wbuf{2 * (m % 2)}"
        wgn = f"wbuf{2 * (m % 2) + 1}"
        wv1 = self.wdma(wi[:], self.w_in_ssm[m].rearrange("p k n -> p (k n)"), guard=win)
        wv2 = self.wdma(wg[:], self.w_in_gate[m].rearrange("p k n -> p (k n)"), guard=wgn)
        self.wait("tensor", self.s_dv)
        self.wait("tensor", self.s_sc)
        self.wait("vector", self.s_sc)
        self.wait("scalar", self.s_dv)
        for r in range(2):
            self.t(lambda e, r=r: e.matmul(
                pb(5)[0:64, 16 * r:16 * (r + 1)], self.ones_sb[0:1, 0:64],
                AP(self.r32.ap().tensor, 16 * (2 * m + r), [[3072, 1], [1, 16]]),
                skip_group_check=True, start=True, stop=True))
        self._retro("tensor", self.s_pe, 1)
        self.wait("vector", self.s_pe)
        self.v(lambda e: e.tensor_copy(self.dpcol[0:64, 0:32], pb(5)[0:64, 0:32]), inc=1)
        self.wait("tensor", self.s_dv)
        stm = self.states_all[0:64, 128 * m:128 * (m + 1)]
        if half == 0:
            self.v(lambda e: e.memset(self.states_bf[:], 0.0))
            self.v(lambda e, stm=stm: e.memset(stm, 0.0), inc=1)
            self.wait("tensor", self.s_dv)
            self.wait("scalar", self.s_dv)
        else:
            dpv7 = AP(self.dpcol.ap().tensor, 7, [[32, 64], [16, 2], [0, 64]])
            stv = AP(stm.tensor, stm.offset, [list(x) for x in stm.ap[:1]] + [[64, 2], [1, 64]])
            self.v(lambda e, stv=stv, dpv7=dpv7: e.tensor_tensor(stv, stv, dpv7, ALU.mult), inc=1)
            self.wait("vector", self.s_dv)
            self.v(lambda e, stv=stv, dpv7=dpv7: e.tensor_tensor(stv, stv, dpv7, ALU.mult), inc=1)
            self.wait("scalar", self.s_dv)
            self.sc(lambda e, stm=stm: e.tensor_copy(self.states_bf[:], stm) if False else
                    e.activation(out=self.states_bf[:], in_=stm, func=AF.Copy), inc=1)
            self.wait("tensor", self.s_sc)
        if self.sub < 11:
            return
        if half == 0:
            self.wait("tensor", self.s_w, wv1)
            for k in range(NK):
                self.t(lambda e, k=k: e.matmul(
                    pb(4)[:, 0:3], wi[:, 128 * k:128 * (k + 1)],
                    self.xm_halo[:, 128 * k + 125:128 * k + 128],
                    skip_group_check=True, start=(k == 0), stop=(k == NK - 1)))
            self._retro("tensor", self.s_pe, 1)
            self.wait("vector", self.s_pe)
            self.v(lambda e: e.tensor_copy(self.scr_in[:, 0:3], pb(4)[:, 0:3]), inc=1)
            self.wait("tensor", self.s_dv)
        else:
            self.v(lambda e: e.tensor_copy(self.scr_in[:, 0:3], self.tail3[:, 4 * m:4 * m + 3]), inc=1)
        self.wait("tensor", self.s_w, wv1)
        for k in range(NK):
            for n2 in range(2):
                self.t(lambda e, k=k, n2=n2: e.matmul(
                    pb(n2)[:], wi[:, 128 * k:128 * (k + 1)], self.rhs_xm(k, n2),
                    skip_group_check=True, start=(k == 0), stop=(k == NK - 1)))
        self._retro("tensor", self.s_pe, 1)
        self.free_buf(win)
        self.wait("vector", self.s_pe)
        for n2 in range(2):
            self.v(lambda e, n2=n2: e.tensor_copy(self.scr_in[:, 3 + 512 * n2:3 + 512 * (n2 + 1)],
                                                  pb(n2)[:]))
        self._retro("vector", self.s_dv, 1)
        self.wait("tensor", self.s_dv)
        if half == 0:
            self.v(lambda e: e.tensor_copy(self.tail3[:, 4 * m:4 * m + 3],
                                           self.scr_in[:, TH:TH + 3]))
        self.v(lambda e: e.tensor_scalar(self.conv_acc[:], self.scr_in[:, 3:3 + TH],
                                         self.convT[:, 4 * m + 3:4 * m + 4],
                                         self.cbT[:, m:m + 1], ALU.mult, ALU.add))
        for kk in range(3):
            self.v(lambda e, kk=kk: e.scalar_tensor_tensor(
                out=self.conv_acc[:], in0=self.scr_in[:, kk:kk + TH],
                scalar=self.convT[:, 4 * m + kk:4 * m + kk + 1],
                in1=self.conv_acc[:], op0=ALU.mult, op1=ALU.add))
        self._retro("vector", self.s_dv, 1)
        self.wait("scalar", self.s_dv)
        self.sc(lambda e: e.activation(out=self.scr_in[:, 0:1024], in_=self.conv_acc[:], func=AF.Silu), inc=1)
        if self.debug and half == 0 and m == 0:
            self.wait("sync", self.s_sc)
            self.odma(self.dbg["d_h"], self.scr_in[:, 0:1024])
        if self.sub < 12:
            return
        self.wait("tensor", self.s_w, wv2)
        for k in range(NK):
            for n2 in range(2):
                self.t(lambda e, k=k, n2=n2: e.matmul(
                    pb(n2)[:], wg[:, 128 * k:128 * (k + 1)], self.rhs_xm(k, n2),
                    skip_group_check=True, start=(k == 0), stop=(k == NK - 1)))
        self._retro("tensor", self.s_pe, 1)
        self.free_buf(wgn)
        self.wait("scalar", self.s_pe)
        for n2 in range(2):
            self.sc(lambda e, n2=n2: e.activation(out=self.gate_sb[:, 512 * n2:512 * (n2 + 1)],
                                                  in_=pb(n2)[:], func=AF.Silu), inc=(n2 == 1))
        sc_gate = self.s_sc.v
        if self.debug and half == 0 and m == 0:
            self.wait("sync", self.s_sc)
            self.odma(self.dbg["d_gate"], self.gate_sb[:])
        if self.sub < 13:
            return
        self.wait("vector", self.s_sc, sc_gate)
        self.v(lambda e: e.tensor_copy(self.w_ssd[:, 0:1024], self.scr_in[:, 0:1024]), inc=1)
        self.wait("tensor", self.s_dv)
        self.wait("tensor", self.s_sc, sc_gate)
        for tl in range(8):
            self.t(lambda e, tl=tl: e.transpose(pbf(4)[:, 128 * tl:128 * (tl + 1)],
                                                self.w_ssd[:, 128 * tl:128 * (tl + 1)],
                                                self.eye_b[:]))
        self._retro("tensor", self.s_pe, 1)
        self.wait("vector", self.s_pe)
        hh = 8 * half
        in0 = AP(self.P[4].ap().bitcast(BF16).tensor, self.P[4].ap().bitcast(BF16).offset,
                 [[1024, 128], [128, 8], [64, 2], [1, 64]])
        in1 = AP(self.dt_tm.ap().tensor, 32 * hh + 2 * m, [[512, 128], [32, 8], [1, 2], [0, 64]])
        outx = AP(self.xdtT.ap().tensor, 0, [[1024, 128], [128, 8], [64, 2], [1, 64]])
        self.v(lambda e, in0=in0, in1=in1, outx=outx:
               e.tensor_tensor(outx, in0, in1, ALU.mult), inc=1)
        self.wait("tensor", self.s_dv)
        benout = AP(self.ben_tm.ap().tensor, 0, [[1024, 128], [128, 8], [64, 2], [1, 64]])
        bin0 = AP(self.B_tm.ap().tensor, 64 * hh, [[1024, 128], [64, 8], [0, 2], [1, 64]])
        bin1 = AP(self.eneg.ap().tensor, 32 * hh + 2 * m, [[512, 128], [32, 8], [1, 2], [0, 64]])
        self.v(lambda e, o=benout, a=bin0, b=bin1: e.tensor_tensor(o, a, b, ALU.mult))
        cseout = AP(self.cse_tm.ap().tensor, 0, [[1024, 128], [128, 8], [64, 2], [1, 64]])
        cin0 = AP(self.C_tm.ap().tensor, 64 * hh, [[1024, 128], [64, 8], [0, 2], [1, 64]])
        cin1 = AP(self.epos.ap().tensor, 32 * hh + 2 * m, [[512, 128], [32, 8], [1, 2], [0, 64]])
        self.v(lambda e, o=cseout, a=cin0, b=cin1: e.tensor_tensor(o, a, b, ALU.mult), inc=1)
        self.wait("tensor", self.s_dv)
        for tl in range(8):
            for r in range(2):
                self.t(lambda e, tl=tl, r=r: e.transpose(
                    pbf(6 + tl // 4)[0:64, 128 * (2 * (tl % 4) + r):128 * (2 * (tl % 4) + r + 1)],
                    self.ben_tm[:, 128 * tl + 64 * r:128 * tl + 64 * (r + 1)],
                    self.eye_b[:]))
        self._retro("tensor", self.s_pe, 1)
        self.wait("vector", self.s_pe)
        self.v(lambda e: e.tensor_copy(self.ben_fm[0:64, 0:1024], pbf(6)[0:64, :]))
        self.v(lambda e: e.tensor_copy(self.ben_fm[0:64, 1024:2048], pbf(7)[0:64, :]), inc=1)
        self.wait("tensor", self.s_dv)
        for tl in range(8):
            for r in range(2):
                self.t(lambda e, tl=tl, r=r: e.transpose(
                    pbf(6 + tl // 4)[0:64, 128 * (2 * (tl % 4) + r):128 * (2 * (tl % 4) + r + 1)],
                    self.cse_tm[:, 128 * tl + 64 * r:128 * tl + 64 * (r + 1)],
                    self.eye_b[:]))
        self._retro("tensor", self.s_pe, 1)
        self.wait("vector", self.s_pe)
        self.v(lambda e: e.tensor_copy(self.cse_fm[0:64, 0:1024], pbf(6)[0:64, :]))
        self.v(lambda e: e.tensor_copy(self.cse_fm[0:64, 1024:2048], pbf(7)[0:64, :]), inc=1)
        self.wait("tensor", self.s_dv)
        if self.sub < 14:
            return
        for tl in range(8):
            for r in range(2):
                s = 2 * tl + r
                off = 128 * (2 * tl + r)
                self.t(lambda e, s=s, off=off: e.matmul(
                    pb(s // 4)[0:64, 128 * (s % 4):128 * (s % 4 + 1)],
                    self.ben_fm[0:64, off:off + 64],
                    self.cse_fm[0:64, off:off + 128],
                    skip_group_check=True, start=True, stop=True))
                self.t(lambda e, s=s, off=off: e.matmul(
                    pb(s // 4)[64:128, 128 * (s % 4) + 64:128 * (s % 4 + 1)],
                    self.ben_fm[0:64, off + 64:off + 128],
                    self.cse_fm[0:64, off + 64:off + 128],
                    skip_group_check=True, start=True, stop=True))
        self._retro("tensor", self.s_pe, 1)
        if self.sub < 15:
            return
        self.wait("vector", self.s_pe)
        self.v(lambda e: e.memset(AP(self.w_ssd.ap().tensor, 64 * 2048,
                                     [[2048, 64], [128, 16], [1, 64]]), 0.0))
        for b in range(4):
            in1 = AP(self.lt_sb.ap().tensor, 0, [[128, 64], [0, 4], [1, 128]])
            outw = AP(self.w_ssd.ap().tensor, 512 * b, [[2048, 64], [128, 4], [1, 128]])
            in0 = AP(self.P[b].ap().tensor, self.P[b].ap().offset,
                     [[512, 64], [128, 4], [1, 128]])
            self.v(lambda e, outw=outw, in0=in0, in1=in1: e.tensor_tensor(
                outw, in0, in1, ALU.mult))
            in1b = AP(self.lt_sb.ap().tensor, 64 * 128 + 64, [[128, 64], [0, 4], [1, 64]])
            outwb = AP(self.w_ssd.ap().tensor, 64 * 2048 + 512 * b + 64,
                       [[2048, 64], [128, 4], [1, 64]])
            in0b = AP(self.P[b].ap().tensor, self.P[b].ap().offset + 64 * 512 + 64,
                      [[512, 64], [128, 4], [1, 64]])
            self.v(lambda e, outw=outwb, in0=in0b, in1=in1b: e.tensor_tensor(
                outw, in0, in1, ALU.mult))
        self._retro("vector", self.s_dv, 1)
        self.wait("tensor", self.s_dv)
        if self.sub < 16:
            return
        for tl in range(8):
            for r in range(2):
                self.t(lambda e, tl=tl, r=r: e.matmul(
                    pb(4 + tl // 4)[:, 128 * (tl % 4) + 64 * r:128 * (tl % 4) + 64 * (r + 1)],
                    self.w_ssd[:, 256 * tl + 128 * r:256 * tl + 128 * (r + 1)],
                    self.xdtT[:, 128 * tl + 64 * r:128 * tl + 64 * (r + 1)],
                    skip_group_check=True, start=True, stop=True))
        for tl in range(8):
            tc = 8 * half + tl
            sbank = 2 + (tl % 2)
            for r in range(2):
                off = 128 * (2 * tl + r)
                self.t(lambda e, tl=tl, r=r, off=off: e.matmul(
                    pb(6 + tl // 4)[:, 128 * (tl % 4) + 64 * r:128 * (tl % 4) + 64 * (r + 1)],
                    self.cse_fm[0:64, off:off + 128],
                    self.states_bf[0:64, 64 * r:64 * (r + 1)],
                    skip_group_check=True, start=True, stop=True))
            for r in range(2):
                self.t(lambda e, tl=tl, r=r, sbank=sbank: e.matmul(
                    pb(sbank)[0:64, 64 * r:64 * (r + 1)],
                    self.ben_tm[:, 128 * tl + 64 * r:128 * tl + 64 * (r + 1)],
                    self.xdtT[:, 128 * tl + 64 * r:128 * tl + 64 * (r + 1)],
                    skip_group_check=True, start=True, stop=True))
            self._retro("tensor", self.s_pe, 1)
            pe_now = self.s_pe.v
            self.wait("vector", self.s_pe, pe_now)
            self.v(lambda e, sbank=sbank, stm=stm: e.tensor_tensor(
                self.st_scr[:], pb(sbank)[0:64, 0:128], stm, ALU.add), inc=1)
            self.wait("vector", self.s_dv)
            last = (tc == 15)
            if tc == 7 or last:
                self.v(lambda e, stm=stm: e.tensor_copy(stm, self.st_scr[:]), inc=1)
            else:
                dpv = AP(self.dpcol.ap().tensor, tc, [[32, 64], [16, 2], [0, 64]])
                stv = AP(stm.tensor, stm.offset,
                         [list(x) for x in stm.ap[:1]] + [[64, 2], [1, 64]])
                self.v(lambda e, stv=stv, dpv=dpv: e.tensor_tensor(
                    stv,
                    AP(self.st_scr.ap().tensor, 0, [[128, 64], [64, 2], [1, 64]]),
                    dpv, ALU.mult), inc=1)
                self.wait("vector", self.s_dv)
                self.v(lambda e, stv=stv, dpv=dpv: e.tensor_tensor(
                    stv, stv, dpv, ALU.mult), inc=1)
            if self.debug and half == 0 and m == 0 and self.stage < 5:
                self.wait("sync", self.s_dv)
                self.odma(self.dbg["d_gq"][0:64, 1536 + 128 * tl:1536 + 128 * (tl + 1)],
                          self.st_scr[:])
                self.wait("vector", self.s_out)
            if not last:
                self.wait("scalar", self.s_dv)
                self.sc(lambda e, stm=stm: e.activation(out=self.states_bf[:], in_=stm,
                                                        func=AF.Copy), inc=1)
                self.wait("tensor", self.s_sc)
            else:
                self.wait("tensor", self.s_dv)
        if self.debug and half == 0 and m == 0 and self.stage < 5:
            self.wait("sync", self.s_dv)
            self.wait("sync", self.s_pe)
            self.odma(self.dbg["d_gq"][:, 0:512], self.xdtT.ap().bitcast(F32)[:, 0:512])
            self.odma(self.dbg["d_gq"][:, 512:1024], self.ben_tm.ap().bitcast(F32)[:, 0:512])
            self.odma(self.dbg["d_gq"][:, 1024:1536], self.cse_tm.ap().bitcast(F32)[:, 0:512])
            self.odma(self.dbg["d_yout"][:, 0:2048], self.w_ssd[:])
            self.odma(self.dbg["d_yout"][0:64, 2048:4096], self.ben_fm[:])
            self.odma(self.dbg["d_yout"][0:64, 4096:6144], self.cse_fm[:])
            self.odma(self.dbg["d_gq"][0:64, 2560:2592], self.dpcol[:])
            self.v(lambda e: e.tensor_copy(self.g_q[:, 0:512], self.pb(4)[:]))
            self.v(lambda e: e.tensor_copy(self.g_q[:, 512:1024], self.pb(5)[:]), inc=1)
            self.wait("sync", self.s_dv)
            self.odma(self.dbg["d_yout"][:, 6144:8192], self.g_q.ap().bitcast(BF16)[:, 0:2048])
            self.wait("tensor", self.s_out)
            self.wait("vector", self.s_out)
        if self.sub < 17:
            return
        self.wait("vector", self.s_pe)
        for b in range(2):
            self.v(lambda e, b=b: e.tensor_copy(self.scr_in[:, 512 * b:512 * (b + 1)],
                                                pb(4 + b)[:]))
        for b in range(2):
            self.v(lambda e, b=b: e.tensor_tensor(self.scr_in[:, 512 * b:512 * (b + 1)],
                                                  pb(6 + b)[:],
                                                  self.scr_in[:, 512 * b:512 * (b + 1)],
                                                  ALU.add), inc=(b == 1))
        self.wait("tensor", self.s_dv)
        for tl in range(8):
            self.t(lambda e, tl=tl: e.transpose(
                pb(tl // 4)[:, 128 * (tl % 4):128 * (tl % 4 + 1)],
                self.scr_in[:, 128 * tl:128 * (tl + 1)], self.eye_f[:]))
        self._retro("tensor", self.s_pe, 1)
        self.wait("vector", self.s_pe)
        for b in range(2):
            self.v(lambda e, b=b, m=m: e.tensor_tensor(
                self.bigbuf[:, 1024 * m + 512 * b:1024 * m + 512 * (b + 1)],
                pb(b)[:], self.gate_sb[:, 512 * b:512 * (b + 1)], ALU.mult), inc=(b == 1))
        self.wait("tensor", self.s_dv)
        if self.debug and half == 0 and m == 0 and self.stage < 5:
            self.wait("sync", self.s_dv)
            self.odma(self.dbg["d_gq"][:, 3072:4096], self.scr_in[:, 0:1024])
            self.odma(self.dbg["d_ybf"][:, 0:1024], self.bigbuf[:, 0:1024])
            self.wait("vector", self.s_out)
            self.wait("tensor", self.s_out)

    def w_out_half(self, half):
        pb = self.pb
        if self.debug and half == 0:
            self.wait("sync", self.s_dv)
            self.odma(self.dbg["d_ybf"], self.bigbuf[:, 0:16 * TH])
        for d in range(8):
            wb = self.w2buf[d % 2]
            wv = self.wdma(wb[:, 0:2048], self.w_out_t[d].rearrange("p k n -> p (k n)"),
                           guard=f"w2buf{d % 2}")
            self.wait("tensor", self.s_w, wv)
            if d >= 2:
                self.wait("tensor", self.s_dv)
            bp = 2 * (d % 2)
            for k in range(16):
                for n2 in range(2):
                    self.t(lambda e, k=k, n2=n2, bp=bp, wb=wb: e.matmul(
                        pb(bp + n2)[:], wb[:, 128 * k:128 * (k + 1)],
                        self.bigbuf[:, 1024 * k + 512 * n2:1024 * k + 512 * (n2 + 1)],
                        skip_group_check=True, start=(k == 0), stop=(k == 15)))
            self._retro("tensor", self.s_pe, 1)
            self.free_buf(f"w2buf{d % 2}")
            self.wait("vector", self.s_pe)
            for n2 in range(2):
                self.v(lambda e, d=d, n2=n2, bp=bp: e.tensor_copy(
                    self.y_out[:, 2048 * d + 1024 * half + 512 * n2:
                               2048 * d + 1024 * half + 512 * (n2 + 1)],
                    pb(bp + n2)[:]), inc=(n2 == 1))
        self.wait("tensor", self.s_dv)
        if self.debug and half == 0:
            self.wait("sync", self.s_dv)
            self.odma(self.dbg["d_w0"], self.y_out[:])
            self.wait("vector", self.s_out)
            self.wait("tensor", self.s_out)
        if self.debug and half == 1:
            self.wait("sync", self.s_dv)
            self.odma(self.dbg["d_yout"], self.y_out[:])

    def rhs_xn(self, k, n4=0):
        return AP(self.xm_fm.ap().tensor, 128 * k, [[8192, 128], [1024, 4], [1, 128]])

    def ffn_quarter(self, q):
        pb, pbf = self.pb, self.pbf
        for i in range(4):
            tt = 4 * q + i
            xbuf = self.xtile[i % 2]
            if i >= 2:
                self.wait("sync", self.s_dv)
            ld = self.xdma(xbuf[:], self.x_c[tt * 128:(tt + 1) * 128, :])
            self.wait("vector", self.s_x, ld)
            self.v(lambda e, xb=xbuf, tt=tt: e.tensor_scalar(
                self.xn_t[:], xb[:], self.stats[:, 4 * tt:4 * tt + 1],
                self.stats[:, 4 * tt + 1:4 * tt + 2], ALU.subtract, ALU.mult))
            self.v(lambda e: e.tensor_copy(self.w_ssd[:, 1024:2048], self.xn_t[:]), inc=1)
            self.wait("tensor", self.s_dv)
            if i > 0:
                self.wait("tensor", self.s_dv)
            for k in range(NK):
                self.t(lambda e, k=k: e.transpose(pbf(0)[:, 128 * k:128 * (k + 1)],
                                                  self.w_ssd[:, 1024 + 128 * k:1024 + 128 * (k + 1)],
                                                  self.eye_b[:]))
            self._retro("tensor", self.s_pe, 1)
            self.wait("vector", self.s_pe)
            dst = AP(self.xm_fm.ap().tensor, 1024 * i, [[8192, 128], [128, 8], [1, 128]])
            src = AP(self.P[0].ap().bitcast(BF16).tensor, self.P[0].ap().bitcast(BF16).offset,
                     [[1024, 128], [128, 8], [1, 128]])
            self.v(lambda e, dst=dst, src=src: e.tensor_copy(dst, src), inc=1)
        self.wait("tensor", self.s_dv)
        if self.debug and q == 0:
            self.wait("sync", self.s_dv)
            self.odma(self.dbg["d_xn"], self.xm_fm[:, 0:4096])
            self.wait("tensor", self.s_out)
        for k in range(NK):
            self.t(lambda e, k=k: e.matmul(pb(6)[0:8, :], self.gateW_sb[:, 8 * k:8 * (k + 1)],
                                           self.rhs_xn(k), skip_group_check=True, start=(k == 0), stop=(k == NK - 1)))
        self._retro("tensor", self.s_pe, 1)
        self.wait("scalar", self.s_pe)
        self.sc(lambda e: e.activation(out=self.xdtT.ap().bitcast(F32)[0:8, 0:TQ], in_=pb(6)[0:8, :],
                                       func=AF.Sigmoid), inc=1)
        self.wait("tensor", self.s_sc)
        if self.debug and q == 0:
            self.wait("sync", self.s_sc)
            self.odma(self.dbg["d_top"][:, 0:512], self.xdtT.ap().bitcast(F32)[:, 0:TQ])
            self.wait("tensor", self.s_out)
        tkc = self.tk
        for i in range(4):
            self.t(lambda e, i=i: e.transpose(pb(5)[:, 8 * i:8 * (i + 1)],
                                              self.xdtT.ap().bitcast(F32)[0:8, 128 * i:128 * (i + 1)],
                                              self.eye_f[0:8, 0:8]))
        self._retro("tensor", self.s_pe, 1)
        self.wait("vector", self.s_pe)
        for i in range(4):
            s_tm = pb(5)[:, 8 * i:8 * (i + 1)]
            self.vS(lambda e, s=s_tm: e.tensor_scalar_mul(tkc[:, 0:8], s, self.rs128[:, 0:1]))
            self.vS(lambda e: e.reduce_max(tkc[:, 8:9], tkc[:, 0:8], axis=AX.X))
            self.vS(lambda e: e.tensor_scalar(tkc[:, 16:24], tkc[:, 0:8], tkc[:, 8:9],
                                              self.fconst[:, 4:5],
                                              ALU.is_equal, ALU.mult))
            self.vS(lambda e: e.tensor_add(tkc[:, 24:32], tkc[:, 0:8], tkc[:, 16:24]))
            self.vS(lambda e: e.reduce_max(tkc[:, 9:10], tkc[:, 24:32], axis=AX.X))
            self.vS(lambda e: e.tensor_add(tkc[:, 10:11], tkc[:, 8:9], tkc[:, 9:10]))
            self.vS(lambda e: e.reciprocal(tkc[:, 11:12], tkc[:, 10:11]))
            self.v(lambda e: e.tensor_scalar(tkc[:, 16:24], tkc[:, 0:8], tkc[:, 8:9],
                                             tkc[:, 8:9], ALU.is_equal, ALU.mult))
            self.vS(lambda e: e.tensor_scalar(tkc[:, 24:32], tkc[:, 24:32], tkc[:, 9:10],
                                              tkc[:, 9:10], ALU.is_equal, ALU.mult))
            self.vS(lambda e: e.tensor_add(tkc[:, 16:24], tkc[:, 16:24], tkc[:, 24:32]))
            self.v(lambda e: e.tensor_scalar_mul(tkc[:, 16:24], tkc[:, 16:24],
                                                 tkc[:, 11:12]), inc=1)
            self.wait("tensor", self.s_dv)
            self.t(lambda e, i=i: e.transpose(pb(4)[0:8, 128 * i:128 * (i + 1)],
                                              tkc[:, 16:24], self.eye_f[:]), inc=1)
            self.wait("vector", self.s_pe)
        self.v(lambda e: e.tensor_copy(self.ben_tm.ap().bitcast(F32)[0:8, 0:TQ], pb(4)[0:8, :]), inc=1)
        self.wait("tensor", self.s_dv)
        if self.debug and q == 0:
            self.wait("sync", self.s_dv)
            self.odma(self.dbg["d_top"][:, 512:1024], self.ben_tm.ap().bitcast(F32)[:, 0:TQ])
            self.wait("vector", self.s_out)
            self.wait("gpsimd", self.s_out)
        self.wait("gpsimd", self.s_dv)
        self.wait("gpsimd", self.s_pe)
        self.gp(lambda e: e.dma_start(
            out=AP(self.g_q.ap().tensor, 0, [[4096, 1], [TQ, 8], [1, TQ]]),
            in_=self.ben_tm.ap().bitcast(F32)[0:8, 0:TQ]), inc=16)
        gflat = self.s_gp.v
        for mm in range(4):
            wb = self.wbuf[mm % 2]
            wv = self.wdma(wb[:], self.Ws1_t[mm].rearrange("p k n -> p (k n)"),
                           guard=f"wbuf{mm % 2}")
            self.wait("tensor", self.s_w, wv)
            if mm >= 1:
                self.wait("tensor", self.s_sc)
            bk = mm % 4
            for k in range(NK):
                self.t(lambda e, k=k, bk=bk, wb=wb: e.matmul(
                    pb(bk)[:], wb[:, 128 * k:128 * (k + 1)], self.rhs_xn(k),
                    skip_group_check=True, start=(k == 0), stop=(k == NK - 1)))
            self._retro("tensor", self.s_pe, 1)
            self.free_buf(f"wbuf{mm % 2}")
            self.wait("scalar", self.s_pe)
            shbuf = (self.B_tm if mm < 2 else self.C_tm)
            self.sc(lambda e, mm=mm, bk=bk, shbuf=shbuf: e.activation(
                out=shbuf[:, TQ * (mm % 2):TQ * (mm % 2 + 1)], in_=pb(bk)[:],
                func=AF.Gelu, bias=self.bs1T[:, mm:mm + 1]), inc=1)
        self.wait("tensor", self.s_gp, gflat)
        for e_ in range(NE):
            wb = self.w2buf[e_ % 2]
            wv = self.wdma(wb[:], self.W1_t[e_].rearrange("p m k n -> p (m k n)"),
                           guard=f"w2buf{e_ % 2}")
            rbank = 6 + (e_ % 2)
            self.wait("tensor", self.s_dv)
            self.t(lambda e, e_=e_, rbank=rbank: e.matmul(
                pb(rbank)[:], self.ones_sb[0:1, :],
                AP(self.g_q.ap().tensor, TQ * e_, [[4096, 1], [1, TQ]]),
                skip_group_check=True, start=True, stop=True), inc=1)
            self.wait("tensor", self.s_w, wv)
            for mm in range(4):
                bk = mm % 2 if e_ % 2 == 0 else 2 + mm % 2
                self.wait("tensor", self.s_dv)
                for k in range(NK):
                    self.t(lambda e, mm=mm, k=k, bk=bk, wb=wb: e.matmul(
                        pb(bk)[:], wb[:, 1024 * mm + 128 * k:1024 * mm + 128 * (k + 1)],
                        self.rhs_xn(k), skip_group_check=True, start=(k == 0), stop=(k == NK - 1)))
                self._retro("tensor", self.s_pe, 1)
                if mm == 3:
                    self.free_buf(f"w2buf{e_ % 2}")
                self.wait("scalar", self.s_pe)
                self.wait("scalar", self.s_dv)
                self.sc(lambda e, e_=e_, mm=mm, bk=bk: e.activation(
                    out=self.cse_tm.ap().bitcast(F32), in_=pb(bk)[:], func=AF.Gelu,
                    bias=self.b1T[:, 4 * e_ + mm:4 * e_ + mm + 1]), inc=1)
                self.wait("vector", self.s_sc)
                self.v(lambda e, e_=e_, mm=mm, rbank=rbank: e.tensor_tensor(
                    self.bigbuf[:, TQ * (4 * e_ + mm):TQ * (4 * e_ + mm + 1)],
                    self.cse_tm.ap().bitcast(F32), pb(rbank)[:], ALU.mult), inc=1)
        self.wait("tensor", self.s_dv)
        self.wait("tensor", self.s_sc)
        if self.debug and q == 0:
            self.wait("sync", self.s_dv)
            self.wait("sync", self.s_sc)
            self.odma(self.dbg["d_ybf"], self.bigbuf[:])
            self.odma(self.dbg["d_scores"], self.B_tm.ap().bitcast(F32)[:, 0:512])
            self.wait("tensor", self.s_out)
            self.wait("gpsimd", self.s_out)
        for d in range(8):
            wb = self.w2buf[d % 2]
            wv = self.wdma(wb[:], self.W2_t[d].rearrange("p e k n -> p (e k n)"),
                           guard=f"w2buf{d % 2}")
            ws = self.wsbuf[0] if d % 2 == 0 else None
            wsap = ws[:] if ws is not None else self.w_ssd[:, 0:512]
            wv2 = self.wdma(wsap, self.Ws2_t[d].rearrange("p k n -> p (k n)"),
                            guard=f"wsbuf{d % 2}")
            bk = d % 4
            self.wait("tensor", self.s_w, wv)
            self.wait("tensor", self.s_w, wv2)
            if d >= 4:
                self.wait("tensor", self.s_dv)
            for k in range(4):
                shb = (self.B_tm if k < 2 else self.C_tm)
                self.t(lambda e, k=k, bk=bk, wsap=wsap, shb=shb: e.matmul(
                    pb(bk)[:], wsap[:, 128 * k:128 * (k + 1)],
                    shb[:, TQ * (k % 2):TQ * (k % 2 + 1)],
                    skip_group_check=True, start=(k == 0), stop=False))
            for ee in range(NE):
                for k in range(4):
                    self.t(lambda e, ee=ee, k=k, bk=bk, wb=wb: e.matmul(
                        pb(bk)[:], wb[:, 512 * ee + 128 * k:512 * ee + 128 * (k + 1)],
                        self.bigbuf[:, TQ * (4 * ee + k):TQ * (4 * ee + k + 1)],
                        skip_group_check=True, start=False, stop=False))
            self.t(lambda e, d=d, bk=bk: e.matmul(
                pb(bk)[:], self.b2_sb[0:8, 128 * d:128 * (d + 1)],
                self.ben_tm.ap().bitcast(F32)[0:8, 0:TQ], skip_group_check=True, start=False, stop=True), inc=1)
            self.free_buf(f"w2buf{d % 2}")
            self.free_buf(f"wsbuf{d % 2}")
            self.wait("vector", self.s_pe)
            self.v(lambda e, d=d, bk=bk: e.tensor_scalar(
                self.g_q[:, TQ * d:TQ * (d + 1)], pb(bk)[:],
                self.bs2T[:, 8 + d:9 + d], self.fconst[:, 2:3], ALU.subtract, ALU.mult), inc=1)
        self.wait("tensor", self.s_dv)
        if self.debug and q == 0:
            self.wait("sync", self.s_dv)
            self.odma(self.dbg["d_gq"], self.g_q[:])
        for i in range(4):
            tt = 4 * q + i
            xbuf = self.xtile[i % 2]
            if i >= 2:
                self.wait("sync", self.s_dv)
                self.wait("vector", self.s_out)
            ld = self.xdma(xbuf[:], self.x_c[tt * 128:(tt + 1) * 128, :])
            if i > 0:
                self.wait("tensor", self.s_dv)
                self.wait("tensor", self.s_sc)
            for k in range(NK):
                self.t(lambda e, k=k, tt=tt: e.transpose(
                    pbf(4)[:, 128 * k:128 * (k + 1)],
                    self.y_out[:, 2048 * k + 128 * tt:2048 * k + 128 * (tt + 1)],
                    self.eye_b[:]))
                self.t(lambda e, k=k, i=i: e.transpose(
                    pb(5 + k // 4)[:, 128 * (k % 4):128 * (k % 4 + 1)],
                    self.g_q[:, TQ * k + 128 * i:TQ * k + 128 * (i + 1)],
                    self.eye_f[:]))
            self._retro("tensor", self.s_pe, 1)
            pe_tr = self.s_pe.v
            self.wait("vector", self.s_pe, pe_tr)
            self.wait("scalar", self.s_pe, pe_tr)
            self.v(lambda e: e.tensor_copy(self.conv_acc[:], pbf(4)[:]))
            self.sc(lambda e: e.activation(out=self.scr_in[:, 0:512], in_=pb(5)[:], func=AF.Silu))
            self.sc(lambda e: e.activation(out=self.scr_in[:, 512:1024], in_=pb(6)[:],
                                           func=AF.Silu), inc=1)
            self.wait("vector", self.s_x, ld)
            self.wait("vector", self.s_sc)
            ob = self.gate_sb[:, 0:1024] if i % 2 == 0 else self.BC_fm[:, 0:1024]
            self.v(lambda e, xb=xbuf, tt=tt: e.tensor_scalar(
                self.xn_t[:], xb[:], self.stats[:, 4 * tt:4 * tt + 1],
                self.stats[:, 4 * tt + 1:4 * tt + 2], ALU.subtract, ALU.mult))
            self.v(lambda e: e.tensor_add(self.xn_t[:], self.xn_t[:], self.conv_acc[:]))
            self.v(lambda e: e.tensor_mul(self.xn_t[:], self.xn_t[:], self.scr_in[:, 0:1024]))
            self.v(lambda e, xb=xbuf, ob=ob: e.tensor_add(ob[:], xb[:], self.xn_t[:]), inc=1)
            self.wait("sync", self.s_dv)
            self.odma(self.out_c[tt * 128:(tt + 1) * 128, :], ob[:])


def _host_layout(inputs):
    w = inputs
    w_in = np.ascontiguousarray(w["L0_attn_w_in"])
    lay = {}
    lay["w_in_gate"] = np.ascontiguousarray(
        w_in[:, :2048].reshape(8, 128, 16, 128).transpose(2, 1, 0, 3))
    lay["w_in_ssm"] = np.ascontiguousarray(
        w_in[:, 2048:4096].reshape(8, 128, 16, 128).transpose(2, 1, 0, 3))
    lay["w_in_dt"] = np.ascontiguousarray(
        w_in[:, 4096:4128].reshape(8, 128, 32).transpose(1, 0, 2))
    lay["w_in_bc"] = np.ascontiguousarray(
        w_in[:, 4128:4256].reshape(8, 128, 128).transpose(1, 0, 2))
    lay["w_out_t"] = np.ascontiguousarray(
        w["L0_attn_w_out"].reshape(16, 128, 8, 128).transpose(2, 1, 0, 3))
    lay["W1_t"] = np.ascontiguousarray(
        w["L0_ffn_W1"].reshape(8, 8, 128, 4, 128).transpose(0, 2, 3, 1, 4))
    lay["W2_t"] = np.ascontiguousarray(
        w["L0_ffn_W2"].reshape(8, 4, 128, 8, 128).transpose(3, 2, 0, 1, 4))
    lay["Ws1_t"] = np.ascontiguousarray(
        w["L0_ffn_Ws1"].reshape(8, 128, 4, 128).transpose(2, 1, 0, 3))
    lay["Ws2_t"] = np.ascontiguousarray(
        w["L0_ffn_Ws2"].reshape(4, 128, 8, 128).transpose(2, 1, 0, 3))
    lay["gateW_t"] = np.ascontiguousarray(
        w["L0_ffn_gate_W"].reshape(8, 128, 8).transpose(1, 0, 2))
    lay["b2_in"] = np.ascontiguousarray(w["L0_ffn_b2"])
    lay["b1T_in"] = np.ascontiguousarray(
        w["L0_ffn_b1"].reshape(8, 4, 128).transpose(2, 0, 1).reshape(128, 32))
    lay["bs1T_in"] = np.ascontiguousarray(w["L0_ffn_bs1"].reshape(4, 128).T)
    lay["bs2T_in"] = np.ascontiguousarray(w["L0_ffn_bs2"].reshape(8, 128).T)
    lay["convT_in"] = np.ascontiguousarray(
        w["L0_attn_conv_weight"].reshape(4, 16, 128).transpose(2, 1, 0).reshape(128, 64))
    lay["cbT_in"] = np.ascontiguousarray(w["L0_attn_conv_bias"].reshape(16, 128).T)
    lay["nsT_in"] = np.ascontiguousarray(w["L0_attn_norm_scale"].reshape(8, 128).T)
    lay["dtb_in"] = np.ascontiguousarray(w["L0_attn_dt_bias"].reshape(32, 1))
    lay["alog_in"] = np.ascontiguousarray(w["L0_attn_A_log"].reshape(32, 1))
    lay["rs_in"] = np.ascontiguousarray(np.asarray(w["L0_ffn_route_scale"],
                                                   np.float32).reshape(1, 1))
    return {k: np.asarray(v, np.float32) for k, v in lay.items()}


_CACHED = {}


def kernel(**inputs):
    x = np.ascontiguousarray(np.asarray(inputs["L0_attn_x"], np.float32).reshape(16384, D))
    lay = _host_layout(inputs)
    if "nc" not in _CACHED:
        b = Builder(debug=False)
        _CACHED["nc"] = b.nc
    nc = _CACHED["nc"]
    in_maps = []
    for c in range(8):
        m = dict(lay)
        m["x_c"] = np.ascontiguousarray(x[c * T:(c + 1) * T])
        m["halo_x"] = (np.zeros((128, D), np.float32) if c == 0 else
                       np.ascontiguousarray(x[c * T - 128:c * T]))
        in_maps.append(m)
    res = run_bass_kernel_spmd(nc, in_maps, core_ids=list(range(8)))
    out = np.concatenate([r["out_c"] for r in res.results], axis=0)
    return out.reshape(1, 16384, D)
